# version 01
# baseline (speedup 1.0000x reference)
import sys
from contextlib import ExitStack

import numpy as np

sys.path.insert(0, "/opt/trn_rl_repo")
import concourse.bass as bass
import concourse.mybir as mybir
from concourse.bass_utils import run_bass_kernel_spmd

F32 = mybir.dt.float32
BF16 = mybir.dt.bfloat16
U32 = mybir.dt.uint32
AF = mybir.ActivationFunctionType
ALU = mybir.AluOpType
AX = mybir.AxisListType

P = 128
TC = 2048
NT = TC // P
D = 1024
DC = D // P
H4 = 4096
HC = H4 // P
NHEAD = 8
DK = 128
SCALE = float(DK) ** -0.5


class SemCtr:

    def __init__(self, es, nc, name):
        self.h = es.enter_context(nc.semaphore(name))
        self.n = 0

    def inc(self, inst, k=1):
        inst.then_inc(self.h, k)
        self.n += k
        return self.n



def build_ln():
    nc = bass.Bass("TRN2", target_bir_lowering=False, debug=False)
    xin = nc.dram_tensor("xin", [TC, D], F32, kind="ExternalInput").ap()
    xn = nc.dram_tensor("xn", [TC, D], F32, kind="ExternalOutput").ap()
    with ExitStack() as es:
        e = es.enter_context
        x_sb = e(nc.sbuf_tensor("x_sb", [P, NT * D], F32))
        o_sb = e(nc.sbuf_tensor("o_sb", [P, NT * D], F32))
        s1 = e(nc.sbuf_tensor("s1", [P, NT], F32))
        s2 = e(nc.sbuf_tensor("s2", [P, NT], F32))
        mu = e(nc.sbuf_tensor("mu", [P, NT], F32))
        var = e(nc.sbuf_tensor("var", [P, NT], F32))
        musq = e(nc.sbuf_tensor("musq", [P, NT], F32))
        std = e(nc.sbuf_tensor("std", [P, NT], F32))
        rstd = e(nc.sbuf_tensor("rstd", [P, NT], F32))
        scr = e(nc.sbuf_tensor("scr", [P, D], F32))
        eps_col = e(nc.sbuf_tensor("eps_col", [P, 1], F32))
        ld = SemCtr(es, nc, "ld")
        gp = SemCtr(es, nc, "gp")
        dv = SemCtr(es, nc, "dv")
        ac = SemCtr(es, nc, "ac")
        st = SemCtr(es, nc, "st")
        block = e(nc.Block())

        x_t = xin.rearrange("(t p) d -> p t d", p=P)
        o_t = xn.rearrange("(t p) d -> p t d", p=P)
        x_sb3 = x_sb[:].rearrange("p (t d) -> p t d", t=NT)
        o_sb3 = o_sb[:].rearrange("p (t d) -> p t d", t=NT)

        @block.gpsimd
        def _(g):
            gp.inc(g.memset(eps_col[:], 1e-5))

        @block.sync
        def _(sy):
            ld.inc(sy.dma_start(out=x_sb3, in_=x_t), 16)
            sy.wait_ge(dv.h, 4 + 2 * NT)
            st.inc(sy.dma_start(out=o_t, in_=o_sb3), 16)
            sy.wait_ge(st.h, 16)

        @block.vector
        def _(v):
            v.wait_ge(ld.h, 16)
            for t in range(NT):
                dv.inc(v.reduce_sum(s1[:, t:t + 1], x_sb[:, t * D:(t + 1) * D], axis=AX.X))
            v.wait_ge(dv.h, NT)
            dv.inc(v.tensor_scalar(mu[:], s1[:], 1.0 / D, None, ALU.mult))
            v.wait_ge(dv.h, NT + 1)
            dv.inc(v.tensor_mul(musq[:], mu[:], mu[:]))
            v.wait_ge(ac.h, NT)
            v.wait_ge(dv.h, NT + 2)
            dv.inc(v.scalar_tensor_tensor(var[:], s2[:], 1.0 / D, musq[:],
                                          ALU.mult, ALU.subtract))
            v.wait_ge(ac.h, NT + 1)
            dv.inc(v.reciprocal(rstd[:], std[:]))
            v.wait_ge(dv.h, NT + 4)
            for t in range(NT):
                dv.inc(v.tensor_scalar(o_sb[:, t * D:(t + 1) * D],
                                       x_sb[:, t * D:(t + 1) * D],
                                       mu[:, t:t + 1], rstd[:, t:t + 1],
                                       ALU.subtract, ALU.mult))

        @block.scalar
        def _(s):
            s.wait_ge(ld.h, 16)
            for t in range(NT):
                ac.inc(s.activation(scr[:], x_sb[:, t * D:(t + 1) * D], AF.Square,
                                    accum_out=s2[:, t:t + 1]))
            s.wait_ge(dv.h, NT + 3)
            s.wait_ge(gp.h, 1)
            ac.inc(s.activation(std[:], var[:], AF.Sqrt, bias=eps_col[:]))
    return nc



def build_proj0():
    nc = bass.Bass("TRN2", target_bir_lowering=False, debug=False)
    xnfm = nc.dram_tensor("xnfm", [D, TC], F32, kind="ExternalInput").ap()
    wqT = nc.dram_tensor("wqT", [D, D], F32, kind="ExternalInput").ap()
    wkT = nc.dram_tensor("wkT", [D, D], F32, kind="ExternalInput").ap()
    wvT = nc.dram_tensor("wvT", [D, D], F32, kind="ExternalInput").ap()
    wg1T = nc.dram_tensor("wg1T", [D, 128], F32, kind="ExternalInput").ap()
    wg2T = nc.dram_tensor("wg2T", [128, D], F32, kind="ExternalInput").ap()
    q_out = nc.dram_tensor("q", [D, TC], BF16, kind="ExternalOutput").ap()
    a_out = nc.dram_tensor("a", [D, TC], BF16, kind="ExternalOutput").ap()
    v_out = nc.dram_tensor("v", [TC, D], BF16, kind="ExternalOutput").ap()
    sg_out = nc.dram_tensor("sg", [D, TC], BF16, kind="ExternalOutput").ap()

    with ExitStack() as es:
        e = es.enter_context
        xn_sb = e(nc.sbuf_tensor("xn_sb", [P, DC * TC], BF16))
        wq_sb = e(nc.sbuf_tensor("wq_sb", [P, DC * D], BF16))
        wk_sb = e(nc.sbuf_tensor("wk_sb", [P, DC * D], BF16))
        wv_sb = e(nc.sbuf_tensor("wv_sb", [P, DC * D], BF16))
        wg1_sb = e(nc.sbuf_tensor("wg1_sb", [P, DC * 128], BF16))
        wg2_sb = e(nc.sbuf_tensor("wg2_sb", [P, D], BF16))
        q_sb = e(nc.sbuf_tensor("q_sb", [P, 2 * TC], BF16))
        a_sb = e(nc.sbuf_tensor("a_sb", [P, 2 * TC], BF16))
        sg_sb = e(nc.sbuf_tensor("sg_sb", [P, 2 * TC], BF16))
        v_sb = e(nc.sbuf_tensor("v_sb", [P, NT * D], BF16))
        g1_sb = e(nc.sbuf_tensor("g1_sb", [P, TC], BF16))
        ps = [e(nc.psum_tensor(f"ps{i}", [P, 512], F32)) for i in range(4)]
        ld = SemCtr(es, nc, "ld")
        pe = SemCtr(es, nc, "pe")
        dv = SemCtr(es, nc, "dv")
        ac = SemCtr(es, nc, "ac")
        st = SemCtr(es, nc, "st")
        block = e(nc.Block())

        @block.gpsimd
        def _(g):
            ld.inc(g.dma_start(out=xn_sb[:].rearrange("p (c t) -> p c t", c=DC), in_=xnfm.rearrange("(c p) t -> p c t", p=P)), 16)
            ld.inc(g.dma_start(out=wq_sb[:].rearrange("p (c o) -> p c o", c=DC), in_=wqT.rearrange("(c p) o -> p c o", p=P)), 16)
            ld.inc(g.dma_start(out=wk_sb[:].rearrange("p (c o) -> p c o", c=DC), in_=wkT.rearrange("(c p) o -> p c o", p=P)), 16)
            ld.inc(g.dma_start(out=wv_sb[:].rearrange("p (c o) -> p c o", c=DC), in_=wvT.rearrange("(c p) o -> p c o", p=P)), 16)
            ld.inc(g.dma_start(out=wg1_sb[:].rearrange("p (c o) -> p c o", c=DC), in_=wg1T.rearrange("(c p) o -> p c o", p=P)), 16)
            ld.inc(g.dma_start(out=wg2_sb[:], in_=wg2T), 16)

        tasks = ([("q", o, t) for o in range(DC) for t in range(4)]
                 + [("a", o, t) for o in range(DC) for t in range(4)]
                 + [("v", t, hf) for t in range(NT) for hf in range(2)]
                 + [("g1", t) for t in range(4)]
                 + [("sg", o, t) for o in range(DC) for t in range(4)])
        ekind = {"q": "ac", "a": "ac", "sg": "ac", "v": "dv", "g1": "dv"}
        evac_no = []
        nac = ndv = 0
        g1_done_dv = None
        for k in tasks:
            if ekind[k[0]] == "ac":
                nac += 1
                evac_no.append(("ac", nac))
            else:
                ndv += 1
                evac_no.append(("dv", ndv))
                if k[0] == "g1" and k[1] == 3:
                    g1_done_dv = ndv
        def q_dma_no(o):
            return o + 1

        def a_dma_no(o):
            return 8 + o + 1

        v_dma_no = 17

        def sg_dma_no(o):
            return 17 + o + 1

        @block.tensor
        def _(t_):
            t_.wait_ge(ld.h, 96)
            for i, task in enumerate(tasks):
                pst = ps[i % 4]
                if i >= 4:
                    eng, cval = evac_no[i - 4]
                    t_.wait_ge(dv.h if eng == "dv" else ac.h, cval)
                kind = task[0]
                if kind in ("q", "a"):
                    _, o, tt = task
                    w = wq_sb if kind == "q" else wk_sb
                    for cc in range(DC):
                        inst = t_.matmul(pst[:, 0:512],
                                         w[:, cc * D + o * P:cc * D + (o + 1) * P],
                                         xn_sb[:, cc * TC + tt * 512:cc * TC + (tt + 1) * 512],
                                         start=(cc == 0), stop=(cc == DC - 1))
                    pe.inc(inst)
                elif kind == "v":
                    _, tt, hf = task
                    for cc in range(DC):
                        inst = t_.matmul(pst[:, 0:512],
                                         xn_sb[:, cc * TC + tt * P:cc * TC + (tt + 1) * P],
                                         wv_sb[:, cc * D + hf * 512:cc * D + (hf + 1) * 512],
                                         start=(cc == 0), stop=(cc == DC - 1))
                    pe.inc(inst)
                elif kind == "g1":
                    _, tt = task
                    for cc in range(DC):
                        inst = t_.matmul(pst[:, 0:512],
                                         wg1_sb[:, cc * 128:(cc + 1) * 128],
                                         xn_sb[:, cc * TC + tt * 512:cc * TC + (tt + 1) * 512],
                                         start=(cc == 0), stop=(cc == DC - 1))
                    pe.inc(inst)
                else:
                    _, o, tt = task
                    t_.wait_ge(dv.h, g1_done_dv)
                    pe.inc(t_.matmul(pst[:, 0:512],
                                     wg2_sb[:, o * P:(o + 1) * P],
                                     g1_sb[:, tt * 512:(tt + 1) * 512],
                                     start=True, stop=True))

        @block.scalar
        def _(s):
            for i, task in enumerate(tasks):
                if ekind[task[0]] != "ac":
                    continue
                pst = ps[i % 4]
                kind, o, tt = task
                buf = {"q": q_sb, "a": a_sb, "sg": sg_sb}[kind]
                fn = {"q": AF.Silu, "a": AF.Exp, "sg": AF.Silu}[kind]
                if tt == 0 and o >= 2:
                    dn = {"q": q_dma_no, "a": a_dma_no, "sg": sg_dma_no}[kind](o - 2)
                    s.wait_ge(st.h, 16 * dn)
                s.wait_ge(pe.h, i + 1)
                ac.inc(s.activation(buf[:, (o % 2) * TC + tt * 512:(o % 2) * TC + (tt + 1) * 512],
                                    pst[:, 0:512], fn))

        @block.vector
        def _(v):
            for i, task in enumerate(tasks):
                if ekind[task[0]] != "dv":
                    continue
                pst = ps[i % 4]
                v.wait_ge(pe.h, i + 1)
                if task[0] == "v":
                    _, tt, hf = task
                    dv.inc(v.tensor_copy(v_sb[:, tt * D + hf * 512:tt * D + (hf + 1) * 512],
                                         pst[:, 0:512]))
                else:
                    _, tt = task
                    dv.inc(v.tensor_copy(g1_sb[:, tt * 512:(tt + 1) * 512], pst[:, 0:512]))

        @block.sync
        def _(sy):
            q3 = q_out.rearrange("(c p) t -> p c t", p=P)
            a3 = a_out.rearrange("(c p) t -> p c t", p=P)
            sg3 = sg_out.rearrange("(c p) t -> p c t", p=P)
            for o in range(DC):
                sy.wait_ge(ac.h, 4 * (o + 1))
                st.inc(sy.dma_start(out=q3[:, o:o + 1, :],
                                    in_=q_sb[:, (o % 2) * TC:(o % 2 + 1) * TC].unsqueeze(1)), 16)
            for o in range(DC):
                sy.wait_ge(ac.h, 32 + 4 * (o + 1))
                st.inc(sy.dma_start(out=a3[:, o:o + 1, :],
                                    in_=a_sb[:, (o % 2) * TC:(o % 2 + 1) * TC].unsqueeze(1)), 16)
            sy.wait_ge(dv.h, 32)
            st.inc(sy.dma_start(out=v_out.rearrange("(t p) d -> p t d", p=P),
                                in_=v_sb[:].rearrange("p (t d) -> p t d", t=NT)), 16)
            for o in range(DC):
                sy.wait_ge(ac.h, 64 + 4 * (o + 1))
                st.inc(sy.dma_start(out=sg3[:, o:o + 1, :],
                                    in_=sg_sb[:, (o % 2) * TC:(o % 2 + 1) * TC].unsqueeze(1)), 16)
            sy.wait_ge(st.h, 16 * 25)
    return nc



def build_ploc():
    nc = bass.Bass("TRN2", target_bir_lowering=False, debug=False)
    a_fm = nc.dram_tensor("a_fm", [D, TC], BF16, kind="ExternalInput").ap()
    a_tm = nc.dram_tensor("a_tm", [TC, D], BF16, kind="ExternalInput").ap()
    v_tm = nc.dram_tensor("v_tm", [TC, D], BF16, kind="ExternalInput").ap()
    ploc = nc.dram_tensor("ploc", [D, DK], F32, kind="ExternalOutput").ap()
    zloc = nc.dram_tensor("zloc", [P, NHEAD], F32, kind="ExternalOutput").ap()

    with ExitStack() as es:
        e = es.enter_context
        afm_sb = e(nc.sbuf_tensor("afm_sb", [P, DC * TC], BF16))
        atm_sb = e(nc.sbuf_tensor("atm_sb", [P, NT * D], BF16))
        vtm_sb = e(nc.sbuf_tensor("vtm_sb", [P, NT * D], BF16))
        p_sb = e(nc.sbuf_tensor("p_sb", [P, NHEAD * DK], F32))
        z_sb = e(nc.sbuf_tensor("z_sb", [P, NHEAD], F32))
        zrow = e(nc.sbuf_tensor("zrow", [1, 512], BF16))
        psP = [e(nc.psum_tensor(f"psP{i}", [P, 512], F32)) for i in range(2)]
        ld = SemCtr(es, nc, "ld")
        pe = SemCtr(es, nc, "pe")
        dv = SemCtr(es, nc, "dv")
        gp = SemCtr(es, nc, "gp")
        st = SemCtr(es, nc, "st")
        block = e(nc.Block())

        @block.gpsimd
        def _(g):
            gp.inc(g.memset(zrow[:], 0.0))
            ld.inc(g.dma_start(out=afm_sb[:].rearrange("p (c t) -> p c t", c=DC), in_=a_fm.rearrange("(c p) t -> p c t", p=P)), 16)
            ld.inc(g.dma_start(out=atm_sb[:].rearrange("p (t d) -> p t d", t=NT), in_=a_tm.rearrange("(t p) d -> p t d", p=P)), 16)
            ld.inc(g.dma_start(out=vtm_sb[:].rearrange("p (t d) -> p t d", t=NT), in_=v_tm.rearrange("(t p) d -> p t d", p=P)), 16)

        @block.tensor
        def _(t_):
            t_.wait_ge(gp.h, 1)
            for b in range(2):
                pe.inc(t_.matmul(psP[b][:, 0:512], zrow[0:1, 0:P], zrow[0:1, 0:512],
                                 start=True, stop=False))
            t_.wait_ge(ld.h, 48)
            for ci in range(NT):
                for h in range(NHEAD):
                    inst = t_.matmul(psP[h // 4][:, (h % 4) * DK:(h % 4 + 1) * DK],
                                     atm_sb[:, ci * D + h * DK:ci * D + (h + 1) * DK],
                                     vtm_sb[:, ci * D + h * DK:ci * D + (h + 1) * DK],
                                     start=False,
                                     stop=(ci == NT - 1 and h % 4 == 3))
                    pe.inc(inst)

        @block.vector
        def _(v):
            v.wait_ge(pe.h, 2 + NT * NHEAD)
            dv.inc(v.tensor_copy(p_sb[:, 0:512], psP[0][:, 0:512]))
            dv.inc(v.tensor_copy(p_sb[:, 512:1024], psP[1][:, 0:512]))
            v.wait_ge(ld.h, 48)
            for h in range(NHEAD):
                dv.inc(v.reduce_sum(z_sb[:, h:h + 1], afm_sb[:, h * TC:(h + 1) * TC], axis=AX.X))

        @block.sync
        def _(sy):
            sy.wait_ge(dv.h, 2 + NHEAD)
            st.inc(sy.dma_start(out=ploc.rearrange("(h k) v -> k h v", k=P), in_=p_sb[:].rearrange("k (h v) -> k h v", h=NHEAD)), 16)
            st.inc(sy.dma_start(out=zloc, in_=z_sb[:]), 16)
            sy.wait_ge(st.h, 32)
    return nc



def build_attn():
    nc = bass.Bass("TRN2", target_bir_lowering=False, debug=False)
    q_in = nc.dram_tensor("q", [D, TC], BF16, kind="ExternalInput").ap()
    afm_in = nc.dram_tensor("a_fm", [D, TC], BF16, kind="ExternalInput").ap()
    atm_in = nc.dram_tensor("a_tm", [TC, D], BF16, kind="ExternalInput").ap()
    vtm_in = nc.dram_tensor("v_tm", [TC, D], BF16, kind="ExternalInput").ap()
    sg_in = nc.dram_tensor("sg", [D, TC], BF16, kind="ExternalInput").ap()
    gn_in = nc.dram_tensor("gnorm_pc", [P, DC], F32, kind="ExternalInput").ap()
    wo_in = nc.dram_tensor("woT", [D, D], F32, kind="ExternalInput").ap()
    pp_in = nc.dram_tensor("pprev", [3, D, DK], F32, kind="ExternalInput").ap()
    zp_in = nc.dram_tensor("zprev", [3, P, NHEAD], F32, kind="ExternalInput").ap()
    attn_out = nc.dram_tensor("attn0", [TC, D], F32, kind="ExternalOutput").ap()

    NIT = NT * NHEAD

    with ExitStack() as es:
        e = es.enter_context
        o_sb = e(nc.sbuf_tensor("o_sb", [P, DC * TC], BF16))
        P_sb = e(nc.sbuf_tensor("P_sb", [P, NHEAD * DK], BF16))
        pb3 = e(nc.sbuf_tensor("pb3", [P, 3 * NHEAD * DK], F32))
        base = e(nc.sbuf_tensor("base", [P, NHEAD * DK], F32))
        zb3 = e(nc.sbuf_tensor("zb3", [P, 3 * NHEAD], F32))
        zbase = e(nc.sbuf_tensor("zbase", [P, NHEAD], F32))
        zc_sb = e(nc.sbuf_tensor("zc_sb", [P, 2 * NHEAD * DK], F32))
        rz_sb = e(nc.sbuf_tensor("rz_sb", [P, 2 * DK], F32))
        qt_sb = e(nc.sbuf_tensor("qt_sb", [P, 2 * DK], BF16))
        at_sb = e(nc.sbuf_tensor("at_sb", [P, 2 * DK], BF16))
        ones_bf = e(nc.sbuf_tensor("ones_bf", [P, DK], BF16))
        ident = e(nc.sbuf_tensor("ident", [P, DK], BF16))
        mask = e(nc.sbuf_tensor("mask", [P, DK], BF16))
        zeros_f = e(nc.sbuf_tensor("zeros_f", [P, DK], F32))
        ones_col_bf = e(nc.sbuf_tensor("ones_col_bf", [P, 1], BF16))
        ones_row_f = e(nc.sbuf_tensor("ones_row_f", [1, P], F32))
        scale_col = e(nc.sbuf_tensor("scale_col", [P, 1], F32))
        ld = SemCtr(es, nc, "ld")
        ld2 = SemCtr(es, nc, "ld2")
        gp = SemCtr(es, nc, "gp")
        pe = SemCtr(es, nc, "pe")
        dv = SemCtr(es, nc, "dv")
        ac = SemCtr(es, nc, "ac")
        st = SemCtr(es, nc, "st")

        DV0 = 5
        PE0 = 2

        def peAT(it):
            return PE0 + 3 * it + 1

        def peO(it):
            return PE0 + 3 * it + 2

        def peP(it):
            return PE0 + 3 * it + 3

        def dvScan(it):
            return DV0 + 4 * it + 1

        def dvRecip(it):
            return DV0 + 4 * it + 2

        def dvQt(it):
            return DV0 + 4 * it + 3

        def dvAtm(it):
            return DV0 + 4 * it + 4

        def acOEvac(it):
            ci, h = divmod(it, NHEAD)
            return 16 * ci + h + 1

        def acPEvac(ci, h):
            return 16 * ci + 8 + h + 1

        with ExitStack() as es1:
            e1 = es1.enter_context
            q_sb = e1(nc.sbuf_tensor("q_sb", [P, DC * TC], BF16))
            afm_sb = e1(nc.sbuf_tensor("afm_sb", [P, DC * TC], BF16))
            atm_sb = e1(nc.sbuf_tensor("atm_sb", [P, NT * D], BF16))
            vtm_sb = e1(nc.sbuf_tensor("vtm_sb", [P, NT * D], BF16))
            psP = [e1(nc.psum_tensor(f"psP{i}", [P, 512], F32)) for i in range(2)]
            psA = [e1(nc.psum_tensor(f"psA{i}", [P, 512], F32)) for i in range(2)]
            psO = [e1(nc.psum_tensor(f"psO{i}", [P, 512], F32)) for i in range(2)]
            block = e1(nc.Block("loop"))

            @block.gpsimd
            def _(g):
                gp.inc(g.memset(ones_bf[:], 1.0))
                gp.inc(g.memset(zeros_f[:], 0.0))
                gp.inc(g.memset(ones_col_bf[:], 1.0))
                gp.inc(g.memset(ones_row_f[:], 1.0))
                gp.inc(g.memset(scale_col[:], SCALE))
                g.wait_ge(gp.h, 1)
                gp.inc(g.affine_select(ident[:], ones_bf[:], [[-1, DK]], ALU.is_equal,
                                       0.0, base=0, channel_multiplier=1))
                gp.inc(g.affine_select(mask[:], ones_bf[:], [[1, DK]], ALU.is_ge,
                                       0.0, base=0, channel_multiplier=-1))

            @block.sync
            def _(sy):
                ld.inc(sy.dma_start(out=q_sb[:].rearrange("p (c t) -> p c t", c=DC),
                                    in_=q_in.rearrange("(c p) t -> p c t", p=P)), 16)
                ld.inc(sy.dma_start(out=afm_sb[:].rearrange("p (c t) -> p c t", c=DC),
                                    in_=afm_in.rearrange("(c p) t -> p c t", p=P)), 16)
                ld.inc(sy.dma_start(out=atm_sb[:].rearrange("p (t d) -> p t d", t=NT),
                                    in_=atm_in.rearrange("(t p) d -> p t d", p=P)), 16)
                ld.inc(sy.dma_start(out=vtm_sb[:].rearrange("p (t d) -> p t d", t=NT),
                                    in_=vtm_in.rearrange("(t p) d -> p t d", p=P)), 16)
                ld.inc(sy.dma_start(out=pb3[:].rearrange("k (r h v) -> k r h v", r=3, h=NHEAD),
                                    in_=pp_in.rearrange("r (h k) v -> k r h v", k=P)), 16)
                ld.inc(sy.dma_start(out=zb3[:].rearrange("p (r h) -> p r h", r=3),
                                    in_=zp_in.rearrange("r p h -> p r h")), 16)

            @block.vector
            def _(v):
                v.wait_ge(ld.h, 96)
                dv.inc(v.tensor_add(base[:], pb3[:, 0:1024], pb3[:, 1024:2048]))
                v.wait_ge(dv.h, 1)
                dv.inc(v.tensor_add(base[:], base[:], pb3[:, 2048:3072]))
                v.wait_ge(dv.h, 2)
                dv.inc(v.tensor_copy(P_sb[:], base[:]))
                dv.inc(v.tensor_add(zbase[:], zb3[:, 0:NHEAD], zb3[:, NHEAD:2 * NHEAD]))
                v.wait_ge(dv.h, 4)
                dv.inc(v.tensor_add(zbase[:], zbase[:], zb3[:, 2 * NHEAD:3 * NHEAD]))
                for it in range(NIT):
                    ci, h = divmod(it, NHEAD)
                    par = it % 2
                    cpar = ci % 2
                    if ci == 0:
                        init = zbase[:, h:h + 1]
                        v.wait_ge(dv.h, DV0)
                    zc_out = zc_sb[:, cpar * NHEAD * DK + h * DK:cpar * NHEAD * DK + (h + 1) * DK]
                    if ci > 0:
                        pcpar = 1 - cpar
                        init = zc_sb[:, pcpar * NHEAD * DK + (h + 1) * DK - 1:
                                     pcpar * NHEAD * DK + (h + 1) * DK]
                        v.wait_ge(dv.h, dvScan(it - NHEAD))
                    dv.inc(v.tensor_tensor_scan(
                        zc_out, afm_sb[:, h * TC + ci * DK:h * TC + (ci + 1) * DK],
                        zeros_f[:], init, ALU.add, ALU.add))
                    v.wait_ge(dv.h, dvScan(it))
                    dv.inc(v.reciprocal(rz_sb[:, par * DK:(par + 1) * DK], zc_out))
                    v.wait_ge(dv.h, dvRecip(it))
                    if it >= 2:
                        v.wait_ge(pe.h, peO(it - 2))
                    dv.inc(v.tensor_mul(qt_sb[:, par * DK:(par + 1) * DK],
                                        q_sb[:, h * TC + ci * DK:h * TC + (ci + 1) * DK],
                                        rz_sb[:, par * DK:(par + 1) * DK]))
                    v.wait_ge(pe.h, peAT(it))
                    dv.inc(v.tensor_mul(at_sb[:, par * DK:(par + 1) * DK],
                                        psA[par][:, 0:DK], mask[:]))

            @block.tensor
            def _(t_):
                t_.wait_ge(gp.h, 6)
                t_.wait_ge(dv.h, 3)
                for b in range(2):
                    pe.inc(t_.matmul(psP[b][:, 0:512], ident[:], P_sb[:, b * 512:(b + 1) * 512],
                                     start=True, stop=False))
                t_.wait_ge(ld.h, 96)
                for it in range(NIT):
                    ci, h = divmod(it, NHEAD)
                    par = it % 2
                    t_.wait_ge(dv.h, dvQt(it))
                    if it >= 2:
                        t_.wait_ge(dv.h, dvAtm(it - 2))
                    pe.inc(t_.matmul(psA[par][:, 0:DK],
                                     afm_sb[:, h * TC + ci * DK:h * TC + (ci + 1) * DK],
                                     qt_sb[:, par * DK:(par + 1) * DK],
                                     start=True, stop=True))
                    t_.wait_ge(dv.h, dvAtm(it))
                    if ci > 0:
                        t_.wait_ge(ac.h, acPEvac(ci - 1, h))
                    if it >= 2:
                        t_.wait_ge(ac.h, acOEvac(it - 2))
                    t_.matmul(psO[par][:, 0:DK],
                              vtm_sb[:, ci * D + h * DK:ci * D + (h + 1) * DK],
                              at_sb[:, par * DK:(par + 1) * DK],
                              start=True, stop=False)
                    pe.inc(t_.matmul(psO[par][:, 0:DK],
                                     P_sb[:, h * DK:(h + 1) * DK],
                                     qt_sb[:, par * DK:(par + 1) * DK],
                                     start=False, stop=True))
                    if ci > 0:
                        t_.wait_ge(ac.h, acPEvac(ci - 1, 7))
                    pe.inc(t_.matmul(psP[h // 4][:, (h % 4) * DK:(h % 4 + 1) * DK],
                                     atm_sb[:, ci * D + h * DK:ci * D + (h + 1) * DK],
                                     vtm_sb[:, ci * D + h * DK:ci * D + (h + 1) * DK],
                                     start=False,
                                     stop=(ci == NT - 1 and h % 4 == 3)))

            @block.scalar
            def _(s):
                for ci in range(NT):
                    for h in range(NHEAD):
                        it = ci * NHEAD + h
                        par = it % 2
                        s.wait_ge(pe.h, peO(it))
                        ac.inc(s.activation(o_sb[:, h * TC + ci * DK:h * TC + (ci + 1) * DK],
                                            psO[par][:, 0:DK], AF.Copy,
                                            scale=scale_col[:]))
                    s.wait_ge(pe.h, peP(ci * NHEAD + 7))
                    for h in range(NHEAD):
                        ac.inc(s.activation(P_sb[:, h * DK:(h + 1) * DK],
                                            psP[h // 4][:, (h % 4) * DK:(h % 4 + 1) * DK],
                                            AF.Copy))

        PE_L = PE0 + 3 * NIT
        DV_L = DV0 + 4 * NIT
        AC_L = 16 * NT

        sg_sb = e(nc.sbuf_tensor("sg_sb", [P, DC * TC], BF16))
        wo_sb = e(nc.sbuf_tensor("wo_sb", [P, DC * D], BF16))
        attn_sb = e(nc.sbuf_tensor("attn_sb", [P, NT * D], F32))
        gn_sb = e(nc.sbuf_tensor("gn_sb", [P, DC], F32))
        o2_sb = e(nc.sbuf_tensor("o2_sb", [P, 2 * 512], BF16))
        sgp_sb = e(nc.sbuf_tensor("sgp_sb", [P, 2 * 512], BF16))
        ms_row = e(nc.sbuf_tensor("ms_row", [1, TC], F32))
        sr_row = e(nc.sbuf_tensor("sr_row", [1, TC], F32))
        rstd_row = e(nc.sbuf_tensor("rstd_row", [1, TC], F32))
        psS = [e(nc.psum_tensor(f"psS{i}", [P, 512], F32)) for i in range(2)]
        psR = [e(nc.psum_tensor(f"psR{i}", [P, 512], F32)) for i in range(2)]
        psW = [e(nc.psum_tensor(f"psW{i}", [P, 512], F32)) for i in range(2)]
        block2 = e(nc.Block("fin"))

        def dvO2(j):
            t, c = divmod(j, 8)
            return DV_L + 9 * t + c + 1

        def dvVar(t):
            return DV_L + 9 * t + 9

        DV_RSTD = DV_L + 36 + 1

        def peStat(j):
            return PE_L + j + 1

        def peRep(t):
            return PE_L + 32 + t + 1

        def dvSgp(j):
            return DV_RSTD + 2 * j + 1

        def dvOf(j):
            return DV_RSTD + 2 * j + 2

        DV_OF = DV_RSTD + 64

        def peWo(idx):
            return PE_L + 36 + idx + 1

        def dvWoEvac(idx):
            return DV_OF + idx + 1

        @block2.gpsimd
        def _(g):
            g.wait_ge(pe.h, PE_L)
            g.wait_ge(dv.h, DV_L)
            g.wait_ge(ac.h, AC_L)
            ld2.inc(g.dma_start(out=sg_sb[:].rearrange("p (c t) -> p c t", c=DC),
                                in_=sg_in.rearrange("(c p) t -> p c t", p=P)), 16)
            ld2.inc(g.dma_start(out=wo_sb[:].rearrange("p (c o) -> p c o", c=DC),
                                in_=wo_in.rearrange("(c p) o -> p c o", p=P)), 16)
            ld2.inc(g.dma_start(out=gn_sb[:], in_=gn_in), 16)

        @block2.vector
        def _(v):
            v.wait_ge(ac.h, AC_L)
            for t in range(4):
                for c in range(8):
                    j = t * 8 + c
                    par = j % 2
                    if j >= 2:
                        v.wait_ge(pe.h, peStat(j - 2))
                    src = o_sb[:, c * TC + t * 512:c * TC + (t + 1) * 512]
                    dv.inc(v.tensor_mul(o2_sb[:, par * 512:(par + 1) * 512], src, src))
                v.wait_ge(pe.h, peStat(8 * t + 7))
                dv.inc(v.tensor_scalar(ms_row[0:1, t * 512:(t + 1) * 512],
                                       psS[t % 2][0:1, 0:512], 1.0 / D, 1e-5,
                                       ALU.mult, ALU.add))
            v.wait_ge(ac.h, AC_L + 1)
            dv.inc(v.reciprocal(rstd_row[:], sr_row[:]))
            v.wait_ge(ld2.h, 48)
            for j in range(32):
                t, c = divmod(j, 8)
                par = j % 2
                v.wait_ge(pe.h, peRep(t))
                dv.inc(v.tensor_mul(sgp_sb[:, par * 512:(par + 1) * 512],
                                    sg_sb[:, c * TC + t * 512:c * TC + (t + 1) * 512],
                                    psR[t % 2][:, 0:512]))
                v.wait_ge(dv.h, dvSgp(j))
                dv.inc(v.scalar_tensor_tensor(o_sb[:, c * TC + t * 512:c * TC + (t + 1) * 512],
                                              o_sb[:, c * TC + t * 512:c * TC + (t + 1) * 512],
                                              gn_sb[:, c:c + 1],
                                              sgp_sb[:, par * 512:(par + 1) * 512],
                                              ALU.mult, ALU.mult))
            for idx in range(32):
                v.wait_ge(pe.h, peWo(idx))
                t128, hf = divmod(idx, 2)
                dv.inc(v.tensor_copy(attn_sb[:, t128 * D + hf * 512:t128 * D + (hf + 1) * 512],
                                     psW[idx % 2][:, 0:512]))

        @block2.tensor
        def _(t_):
            for j in range(32):
                t, c = divmod(j, 8)
                par = j % 2
                t_.wait_ge(dv.h, dvO2(j))
                if t >= 2 and c == 0:
                    t_.wait_ge(dv.h, dvVar(t - 2))
                pe.inc(t_.matmul(psS[t % 2][0:1, 0:512], ones_col_bf[:],
                                 o2_sb[:, par * 512:(par + 1) * 512],
                                 start=(c == 0), stop=(c == 7)))
            for t in range(4):
                t_.wait_ge(dv.h, DV_RSTD)
                if t >= 2:
                    t_.wait_ge(dv.h, dvOf(8 * (t - 2) + 7))
                pe.inc(t_.matmul(psR[t % 2][:, 0:512], ones_row_f[:],
                                 rstd_row[0:1, t * 512:(t + 1) * 512],
                                 start=True, stop=True))
            t_.wait_ge(ld2.h, 48)
            for idx in range(32):
                t128, hf = divmod(idx, 2)
                t512 = t128 // 4
                t_.wait_ge(dv.h, dvOf(8 * t512 + 7))
                if idx >= 2:
                    t_.wait_ge(dv.h, dvWoEvac(idx - 2))
                for c in range(DC):
                    inst = t_.matmul(psW[idx % 2][:, 0:512],
                                     o_sb[:, c * TC + t128 * P:c * TC + (t128 + 1) * P],
                                     wo_sb[:, c * D + hf * 512:c * D + (hf + 1) * 512],
                                     start=(c == 0), stop=(c == DC - 1))
                pe.inc(inst)

        @block2.scalar
        def _(s):
            s.wait_ge(dv.h, dvVar(3))
            ac.inc(s.activation(sr_row[:], ms_row[:], AF.Sqrt))

        @block2.sync
        def _(sy):
            sy.wait_ge(dv.h, DV_OF + 32)
            st.inc(sy.dma_start(out=attn_out.rearrange("(t p) d -> p t d", p=P),
                                in_=attn_sb[:].rearrange("p (t d) -> p t d", t=NT)), 16)
            sy.wait_ge(st.h, 16)
    return nc



def build_ffn0():
    nc = bass.Bass("TRN2", target_bir_lowering=False, debug=False)
    x_in = nc.dram_tensor("x", [TC, D], F32, kind="ExternalInput").ap()
    at_in = nc.dram_tensor("attn0", [TC, D], F32, kind="ExternalInput").ap()
    xnfm = nc.dram_tensor("xnfm", [D, TC], F32, kind="ExternalInput").ap()
    wfcT = nc.dram_tensor("wfcT", [D, H4], F32, kind="ExternalInput").ap()
    wpT = nc.dram_tensor("wpT", [H4, D], F32, kind="ExternalInput").ap()
    lnw_in = nc.dram_tensor("lnw_pc", [P, HC], F32, kind="ExternalInput").ap()
    lnb_in = nc.dram_tensor("lnb_pc", [P, HC], F32, kind="ExternalInput").ap()
    x1_out = nc.dram_tensor("x1", [TC, D], F32, kind="ExternalOutput").ap()
    h_scr = nc.dram_tensor("h_scr", [4 * P, HC * 512], BF16)
    bounce = nc.dram_tensor("bounce", [1, TC], F32)

    wfcT3 = wfcT.rearrange("(c p) f -> p c f", p=P)
    wpT3 = wpT.rearrange("(c p) f -> p c f", p=P)

    with ExitStack() as es:
        e = es.enter_context
        wbuf = e(nc.sbuf_tensor("wbuf", [P, H4], F32))
        t1b = e(nc.sbuf_tensor("t1b", [P, H4], BF16))
        t2b = e(nc.sbuf_tensor("t2b", [P, H4], BF16))
        h_sb = e(nc.sbuf_tensor("h_sb", [P, HC * 512], BF16))
        h2_sb = e(nc.sbuf_tensor("h2_sb", [P, 2 * 512], BF16))
        ccrep = e(nc.sbuf_tensor("ccrep", [P, D], F32))
        ga = e(nc.sbuf_tensor("ga", [P, DC], F32))
        gp32 = e(nc.sbuf_tensor("gp32", [P, HC], F32))
        gb = e(nc.sbuf_tensor("gb", [P, 1], F32))
        onecol_f = e(nc.sbuf_tensor("onecol_f", [P, 1], F32))
        ones_col_bf = e(nc.sbuf_tensor("ones_col_bf", [P, 1], BF16))
        ones_row_f = e(nc.sbuf_tensor("ones_row_f", [1, P], F32))
        thr_row = e(nc.sbuf_tensor("thr_row", [1, 1], F32))
        thr_col = e(nc.sbuf_tensor("thr_col", [P, 1], F32))
        nthr_col = e(nc.sbuf_tensor("nthr_col", [P, 1], F32))
        negmu_row = e(nc.sbuf_tensor("negmu_row", [1, TC], F32))
        rstd_row = e(nc.sbuf_tensor("rstd_row", [1, TC], F32))
        musq_s = e(nc.sbuf_tensor("musq_s", [1, 512], F32))
        var_s = e(nc.sbuf_tensor("var_s", [1, 512], F32))
        std_s = e(nc.sbuf_tensor("std_s", [1, 512], F32))
        rstd_cols = e(nc.sbuf_tensor("rstd_cols", [P, NT], F32))
        eps_row = e(nc.sbuf_tensor("eps_row", [1, 1], F32))
        lnw_sb = e(nc.sbuf_tensor("lnw_sb", [P, HC], F32))
        lnb_bf = e(nc.sbuf_tensor("lnb_bf", [P, HC], BF16))
        c_row = e(nc.sbuf_tensor("c_row", [1, D], F32))
        cc_row = e(nc.sbuf_tensor("cc_row", [1, D], F32))
        psMisc = e(nc.psum_tensor("psMisc", [P, 512], F32))
        psS1 = e(nc.psum_tensor("psS1", [P, 512], F32))
        psS2 = e(nc.psum_tensor("psS2", [P, 512], F32))
        psCC = e(nc.psum_tensor("psCC", [P, 512], F32))
        psC = e(nc.psum_tensor("psC", [P, 512], F32))
        psREP = e(nc.psum_tensor("psREP", [P, 512], F32))
        ld = SemCtr(es, nc, "ld")
        ld2 = SemCtr(es, nc, "ld2")
        gp = SemCtr(es, nc, "gp")
        pe = SemCtr(es, nc, "pe")
        dv = SemCtr(es, nc, "dv")
        ac = SemCtr(es, nc, "ac")
        st = SemCtr(es, nc, "st")

        with ExitStack() as es_a:
            e1 = es_a.enter_context
            xn_sb = e1(nc.sbuf_tensor("xn_sb", [P, DC * TC], BF16))
            wfcq = e1(nc.sbuf_tensor("wfcq", [P, DC * H4], BF16))
            psF = [e1(nc.psum_tensor(f"psF{i}", [P, 512], F32)) for i in range(2)]

            blk = nc.Block("e1"); blk.__enter__()

            @blk.gpsimd
            def _(g):
                gp.inc(g.memset(onecol_f[:], 1.0))
                gp.inc(g.memset(ones_col_bf[:], 1.0))
                gp.inc(g.memset(ones_row_f[:], 1.0))
                gp.inc(g.memset(eps_row[:], 1e-5))
                for c in range(DC):
                    if c >= 1:
                        g.wait_ge(dv.h, c)
                    ld.inc(g.dma_start(out=wbuf[:], in_=wfcT3[:, c:c + 1, :]), 16)
                ld2.inc(g.dma_start(out=xn_sb[:].rearrange("p (c t) -> p c t", c=DC),
                                    in_=xnfm.rearrange("(c p) t -> p c t", p=P)), 16)
                ld2.inc(g.dma_start(out=lnw_sb[:], in_=lnw_in), 16)
                ld2.inc(g.dma_start(out=lnb_bf[:], in_=lnb_in), 16)
                for c in range(DC):
                    if c == 0:
                        g.wait_ge(dv.h, 8)
                    else:
                        g.wait_ge(dv.h, 14 + 3 * (c - 1))
                    ld.inc(g.dma_start(out=wbuf[:], in_=wfcT3[:, c:c + 1, :]), 16)

            @blk.vector
            def _(v):
                for c in range(DC):
                    v.wait_ge(ld.h, 16 * (c + 1))
                    dv.inc(v.reduce_sum(ga[:, c:c + 1], wbuf[:], axis=AX.X,
                                        apply_absolute_value=True))
                v.wait_ge(dv.h, 8)
                dv.inc(v.reduce_sum(gb[:], ga[:], axis=AX.X))
                v.wait_ge(pe.h, 1)
                dv.inc(v.tensor_scalar(thr_row[:], psMisc[0:1, 0:1],
                                       0.5 / (H4 * D), 5e-7, ALU.mult, ALU.add))
                v.wait_ge(pe.h, 2)
                dv.inc(v.tensor_copy(thr_col[:], psMisc[:, 0:1]))
                v.wait_ge(dv.h, 11)
                dv.inc(v.tensor_scalar(nthr_col[:], thr_col[:], -1.0, None,
                                       ALU.mult))
                for c in range(DC):
                    v.wait_ge(ld.h, 16 * (9 + c))
                    dv.inc(v.tensor_scalar(t1b[:], wbuf[:], thr_col[:], None,
                                           ALU.is_gt))
                    dv.inc(v.tensor_scalar(t2b[:], wbuf[:], nthr_col[:], None,
                                           ALU.is_lt))
                    v.wait_ge(dv.h, 14 + 3 * c)
                    dv.inc(v.tensor_sub(wfcq[:, c * H4:(c + 1) * H4], t1b[:], t2b[:]))

            @blk.tensor
            def _(t_):
                t_.wait_ge(gp.h, 1)
                t_.wait_ge(dv.h, 9)
                pe.inc(t_.matmul(psMisc[0:1, 0:1], gb[:], onecol_f[:],
                                 start=True, stop=True))
                t_.wait_ge(gp.h, 3)
                t_.wait_ge(dv.h, 10)
                pe.inc(t_.matmul(psMisc[:, 0:1], ones_row_f[:], thr_row[:],
                                 start=True, stop=True))

            E1_dv = 12 + 3 * DC
            E1_pe = 2
            E1_ac = 0
            E1_ld = ld.n

            peN = {}
            n = E1_pe
            for t in range(4):
                for f in range(HC):
                    n += 1
                    peN[("fc", t, f)] = n
                    if f >= 1:
                        n += 1
                        peN[("s1", t, f - 1)] = n
                        n += 1
                        peN[("s2", t, f - 1)] = n
                n += 1
                peN[("s1", t, HC - 1)] = n
                n += 1
                peN[("s2", t, HC - 1)] = n
            E2_pe = n
            acN = {}
            n = E1_ac
            for t in range(4):
                for f in range(HC):
                    n += 1
                    acN[("gelu", t, f)] = n
                n += 1
                acN[("std", t)] = n
            E2_ac = n
            dvN = {}
            n = E1_dv
            for t in range(4):
                for f in range(HC):
                    n += 1
                    dvN[("h2", t, f)] = n
                for k in ("negmu", "musq", "var", "rstd"):
                    n += 1
                    dvN[(k, t)] = n
            E2_dv = n
            E2_st = 64

            blk.__exit__(None, None, None)
            blk2 = nc.Block("e2"); blk2.__enter__()

            @blk2.tensor
            def _(t_):
                t_.wait_ge(ld2.h, 48)
                t_.wait_ge(dv.h, E1_dv)
                for t in range(4):
                    for f in range(HC):
                        pf = t * HC + f
                        if pf >= 2:
                            t2, f2 = divmod(pf - 2, HC)
                            t_.wait_ge(ac.h, acN[("gelu", t2, f2)])
                        for cc in range(DC):
                            inst = t_.matmul(psF[pf % 2][:, 0:512],
                                             wfcq[:, cc * H4 + f * P:cc * H4 + (f + 1) * P],
                                             xn_sb[:, cc * TC + t * 512:cc * TC + (t + 1) * 512],
                                             start=(cc == 0), stop=(cc == DC - 1))
                        pe.inc(inst)
                        if f >= 1:
                            t_.wait_ge(ac.h, acN[("gelu", t, f - 1)])
                            pe.inc(t_.matmul(psS1[0:1, 0:512], ones_col_bf[:],
                                             h_sb[:, (f - 1) * 512:f * 512],
                                             start=(f == 1), stop=False))
                            t_.wait_ge(dv.h, dvN[("h2", t, f - 1)])
                            pe.inc(t_.matmul(psS2[0:1, 0:512], ones_col_bf[:],
                                             h2_sb[:, ((f - 1) % 2) * 512:((f - 1) % 2 + 1) * 512],
                                             start=(f == 1), stop=False))
                    t_.wait_ge(ac.h, acN[("gelu", t, HC - 1)])
                    pe.inc(t_.matmul(psS1[0:1, 0:512], ones_col_bf[:],
                                     h_sb[:, (HC - 1) * 512:HC * 512],
                                     start=False, stop=True))
                    t_.wait_ge(dv.h, dvN[("h2", t, HC - 1)])
                    pe.inc(t_.matmul(psS2[0:1, 0:512], ones_col_bf[:],
                                     h2_sb[:, ((HC - 1) % 2) * 512:((HC - 1) % 2 + 1) * 512],
                                     start=False, stop=True))

            @blk2.scalar
            def _(sc):
                for t in range(4):
                    for f in range(HC):
                        if t >= 1 and f == 0:
                            sc.wait_ge(st.h, 16 * t)
                        sc.wait_ge(pe.h, peN[("fc", t, f)])
                        ac.inc(sc.activation(h_sb[:, f * 512:(f + 1) * 512],
                                             psF[(t * HC + f) % 2][:, 0:512], AF.Gelu))
                    sc.wait_ge(dv.h, dvN[("var", t)])
                    ac.inc(sc.activation(std_s[:], var_s[:], AF.Sqrt,
                                         bias=eps_row[0:1, 0:1]))

            @blk2.vector
            def _(v):
                for t in range(4):
                    for f in range(HC):
                        v.wait_ge(ac.h, acN[("gelu", t, f)])
                        pf = t * HC + f
                        if pf >= 2:
                            t2, f2 = divmod(pf - 2, HC)
                            v.wait_ge(pe.h, peN[("s2", t2, f2)])
                        src = h_sb[:, f * 512:(f + 1) * 512]
                        dv.inc(v.tensor_mul(h2_sb[:, (pf % 2) * 512:(pf % 2 + 1) * 512],
                                            src, src))
                    v.wait_ge(pe.h, peN[("s1", t, HC - 1)])
                    dv.inc(v.tensor_scalar(negmu_row[0:1, t * 512:(t + 1) * 512],
                                           psS1[0:1, 0:512], -1.0 / H4, None, ALU.mult))
                    v.wait_ge(dv.h, dvN[("negmu", t)])
                    if t >= 1:
                        v.wait_ge(ac.h, acN[("std", t - 1)])
                    dv.inc(v.tensor_mul(musq_s[:],
                                        negmu_row[0:1, t * 512:(t + 1) * 512],
                                        negmu_row[0:1, t * 512:(t + 1) * 512]))
                    v.wait_ge(pe.h, peN[("s2", t, HC - 1)])
                    v.wait_ge(dv.h, dvN[("musq", t)])
                    dv.inc(v.scalar_tensor_tensor(var_s[:], psS2[0:1, 0:512],
                                                  1.0 / H4, musq_s[:],
                                                  ALU.mult, ALU.subtract))
                    v.wait_ge(ac.h, acN[("std", t)])
                    dv.inc(v.reciprocal(rstd_row[0:1, t * 512:(t + 1) * 512],
                                        std_s[:]))

            @blk2.sync
            def _(sy):
                for t in range(4):
                    sy.wait_ge(ac.h, acN[("gelu", t, HC - 1)])
                    sy.wait_ge(dv.h, dvN[("h2", t, HC - 1)])
                    st.inc(sy.dma_start(out=h_scr.ap()[t * P:(t + 1) * P, :],
                                        in_=h_sb[:]), 16)
                sy.wait_ge(dv.h, dvN[("rstd", 3)])
                st.inc(sy.dma_start(out=bounce.ap(), in_=rstd_row[:]), 16)
                sy.wait_ge(st.h, st.n)
                st.inc(sy.dma_start(out=rstd_cols[:].unsqueeze(2),
                                    in_=bounce.ap().rearrange("a (t p) -> p t a", p=P),
                                    allow_slow_non_contiguous=True), 16)
            blk2.__exit__(None, None, None)

        wpq = e(nc.sbuf_tensor("wpq", [P, HC * D], BF16))
        xbuf = e(nc.sbuf_tensor("xbuf", [P, 4 * D], F32))
        abuf = e(nc.sbuf_tensor("abuf", [P, 4 * D], F32))
        tmpA = e(nc.sbuf_tensor("tmpA", [P, 2 * 512], F32))
        psJ = [e(nc.psum_tensor(f"psJ{i}", [P, 512], F32)) for i in range(2)]

        DV3 = E2_dv
        PE3 = E2_pe
        LD3 = E1_ld
        DV_G = DV3 + HC
        DV_THRC = DV_G + 3
        DV_NTHR = DV_THRC + 1

        def dvRaw(c):
            return DV_NTHR + 4 * c + 3

        def dvScale(c):
            return DV_NTHR + 4 * c + 4
        DV_TERN = DV_NTHR + 4 * HC
        DV_CCROW = DV_TERN + 2
        DV_CROW = DV_TERN + 4
        DV_CCREP = DV_TERN + 6
        PE_G = PE3 + 2

        def peCC(c):
            return PE_G + 4 * c + 2

        def peC(c):
            return PE_G + 4 * c + 4
        PE_TERN = PE_G + 4 * HC
        PE_CCREP1 = PE_TERN + 1
        PE_CCREP2 = PE_TERN + 2

        blk3 = nc.Block("e3"); blk3.__enter__()

        @blk3.gpsimd
        def _(g):
            g.wait_ge(pe.h, PE3)
            for c in range(HC):
                if c >= 1:
                    g.wait_ge(dv.h, DV3 + c)
                ld.inc(g.dma_start(out=wbuf[:, 0:D], in_=wpT3[:, c:c + 1, :]), 16)
            for c in range(HC):
                if c == 0:
                    g.wait_ge(dv.h, DV_G)
                else:
                    g.wait_ge(dv.h, DV_NTHR + 4 * (c - 1) + 2)
                ld.inc(g.dma_start(out=wbuf[:, 0:D], in_=wpT3[:, c:c + 1, :]), 16)

        @blk3.vector
        def _(v):
            for c in range(HC):
                v.wait_ge(ld.h, LD3 + 16 * (c + 1))
                dv.inc(v.reduce_sum(gp32[:, c:c + 1], wbuf[:, 0:D], axis=AX.X,
                                    apply_absolute_value=True))
            v.wait_ge(dv.h, DV_G)
            dv.inc(v.reduce_sum(gb[:], gp32[:], axis=AX.X))
            v.wait_ge(pe.h, PE_G - 1)
            dv.inc(v.tensor_scalar(thr_row[:], psMisc[0:1, 0:1],
                                   0.5 / (H4 * D), 5e-7, ALU.mult, ALU.add))
            v.wait_ge(pe.h, PE_G)
            dv.inc(v.tensor_copy(thr_col[:], psMisc[:, 0:1]))
            v.wait_ge(dv.h, DV_THRC)
            dv.inc(v.tensor_scalar(nthr_col[:], thr_col[:], -1.0, None, ALU.mult))
            v.wait_ge(ld2.h, 48)
            for c in range(HC):
                v.wait_ge(ld.h, LD3 + 16 * (HC + c + 1))
                dv.inc(v.tensor_scalar(t1b[:, 0:D], wbuf[:, 0:D], thr_col[:], None,
                                       ALU.is_gt))
                dv.inc(v.tensor_scalar(t2b[:, 0:D], wbuf[:, 0:D], nthr_col[:], None,
                                       ALU.is_lt))
                v.wait_ge(dv.h, DV_NTHR + 4 * c + 2)
                dv.inc(v.tensor_sub(wpq[:, c * D:(c + 1) * D], t1b[:, 0:D], t2b[:, 0:D]))
                v.wait_ge(pe.h, peCC(c))
                dv.inc(v.tensor_scalar(wpq[:, c * D:(c + 1) * D],
                                       wpq[:, c * D:(c + 1) * D],
                                       lnw_sb[:, c:c + 1], None, ALU.mult))
            v.wait_ge(pe.h, PE_TERN)
            dv.inc(v.tensor_copy(cc_row[0:1, 0:512], psCC[0:1, 0:512]))
            dv.inc(v.tensor_copy(cc_row[0:1, 512:1024], psS1[0:1, 0:512]))
            dv.inc(v.tensor_copy(c_row[0:1, 0:512], psC[0:1, 0:512]))
            dv.inc(v.tensor_copy(c_row[0:1, 512:1024], psS2[0:1, 0:512]))
            v.wait_ge(pe.h, PE_CCREP1)
            dv.inc(v.tensor_copy(ccrep[:, 0:512], psREP[:, 0:512]))
            v.wait_ge(pe.h, PE_CCREP2)
            dv.inc(v.tensor_copy(ccrep[:, 512:1024], psREP[:, 0:512]))

        @blk3.tensor
        def _(t_):
            t_.wait_ge(dv.h, DV_G + 1)
            pe.inc(t_.matmul(psMisc[0:1, 0:1], gb[:], onecol_f[:],
                             start=True, stop=True))
            t_.wait_ge(dv.h, DV_G + 2)
            pe.inc(t_.matmul(psMisc[:, 0:1], ones_row_f[:], thr_row[:],
                             start=True, stop=True))
            t_.wait_ge(ld2.h, 48)
            for c in range(HC):
                t_.wait_ge(dv.h, dvRaw(c))
                pe.inc(t_.matmul(psCC[0:1, 0:512], lnb_bf[:, c:c + 1],
                                 wpq[:, c * D:c * D + 512],
                                 start=(c == 0), stop=(c == HC - 1)))
                pe.inc(t_.matmul(psS1[0:1, 0:512], lnb_bf[:, c:c + 1],
                                 wpq[:, c * D + 512:(c + 1) * D],
                                 start=(c == 0), stop=(c == HC - 1)))
                t_.wait_ge(dv.h, dvScale(c))
                pe.inc(t_.matmul(psC[0:1, 0:512], ones_col_bf[:],
                                 wpq[:, c * D:c * D + 512],
                                 start=(c == 0), stop=(c == HC - 1)))
                pe.inc(t_.matmul(psS2[0:1, 0:512], ones_col_bf[:],
                                 wpq[:, c * D + 512:(c + 1) * D],
                                 start=(c == 0), stop=(c == HC - 1)))
            t_.wait_ge(dv.h, DV_CCROW)
            pe.inc(t_.matmul(psREP[:, 0:512], ones_row_f[:], cc_row[0:1, 0:512],
                             start=True, stop=True))
            t_.wait_ge(dv.h, DV_CCREP - 1)
            pe.inc(t_.matmul(psREP[:, 0:512], ones_row_f[:], cc_row[0:1, 512:1024],
                             start=True, stop=True))

        PE4 = PE_CCREP2
        DV4 = DV_CCREP
        peJ = {}
        n = PE4
        for t in range(4):
            for idx in range(8):
                n += 1
                peJ[(t, idx)] = n
        dvE = {}
        n = DV4
        for t in range(4):
            for idx in range(8):
                dvE[(t, idx)] = (n + 1, n + 2, n + 3)
                n += 3
        ld4 = {}
        n = ld.n
        for t in range(4):
            ld4[("h", t)] = n + 16
            ld4[("x", t)] = n + 32
            ld4[("a", t)] = n + 48
            n += 48
        stO = {t: E2_st + 32 + 16 * (t + 1) for t in range(4)}

        blk3.__exit__(None, None, None)
        blk4 = nc.Block("e4"); blk4.__enter__()

        @blk4.gpsimd
        def _(g):
            for t in range(4):
                if t == 0:
                    g.wait_ge(st.h, E2_st)
                else:
                    g.wait_ge(pe.h, peJ[(t - 1, 7)])
                ld.inc(g.dma_start(out=h_sb[:],
                                   in_=h_scr.ap()[t * P:(t + 1) * P, :]), 16)
                if t >= 1:
                    g.wait_ge(dv.h, dvE[(t - 1, 7)][2])
                    g.wait_ge(st.h, stO[t - 1])
                ld.inc(g.dma_start(out=xbuf[:].rearrange("p (u d) -> p u d", u=4),
                                   in_=x_in.rearrange("(t p) d -> p t d", p=P)[:, 4 * t:4 * t + 4, :]), 16)
                ld.inc(g.dma_start(out=abuf[:].rearrange("p (u d) -> p u d", u=4),
                                   in_=at_in.rearrange("(t p) d -> p t d", p=P)[:, 4 * t:4 * t + 4, :]), 16)

        @blk4.tensor
        def _(t_):
            t_.wait_ge(dv.h, DV4)
            for t in range(4):
                t_.wait_ge(ld.h, ld4[("a", t)])
                for idx in range(8):
                    tt, half = divmod(idx, 2)
                    tg = 4 * t + tt
                    gi = t * 8 + idx
                    if gi >= 2:
                        t2, i2 = divmod(gi - 2, 8)
                        t_.wait_ge(dv.h, dvE[(t2, i2)][0])
                    for f in range(HC):
                        t_.matmul(psJ[idx % 2][:, 0:512],
                                  h_sb[:, f * 512 + tt * P:f * 512 + (tt + 1) * P],
                                  wpq[:, f * D + half * 512:f * D + (half + 1) * 512],
                                  start=(f == 0), stop=False)
                    pe.inc(t_.matmul(psJ[idx % 2][:, 0:512],
                                     negmu_row[0:1, tg * P:(tg + 1) * P],
                                     c_row[0:1, half * 512:(half + 1) * 512],
                                     start=False, stop=True))

        @blk4.vector
        def _(v):
            v.wait_ge(st.h, E2_st + 32)
            for t in range(4):
                v.wait_ge(ld.h, ld4[("a", t)])
                for idx in range(8):
                    tt, half = divmod(idx, 2)
                    tg = 4 * t + tt
                    par = idx % 2
                    e1n, e2n, e3n = dvE[(t, idx)]
                    v.wait_ge(pe.h, peJ[(t, idx)])
                    dv.inc(v.scalar_tensor_tensor(tmpA[:, par * 512:(par + 1) * 512],
                                                  psJ[par][:, 0:512],
                                                  rstd_cols[:, tg:tg + 1],
                                                  xbuf[:, tt * D + half * 512:tt * D + (half + 1) * 512],
                                                  ALU.mult, ALU.add))
                    v.wait_ge(dv.h, e1n)
                    dv.inc(v.tensor_add(tmpA[:, par * 512:(par + 1) * 512],
                                        tmpA[:, par * 512:(par + 1) * 512],
                                        abuf[:, tt * D + half * 512:tt * D + (half + 1) * 512]))
                    v.wait_ge(dv.h, e2n)
                    dv.inc(v.tensor_add(xbuf[:, tt * D + half * 512:tt * D + (half + 1) * 512],
                                        tmpA[:, par * 512:(par + 1) * 512],
                                        ccrep[:, half * 512:(half + 1) * 512]))

        @blk4.sync
        def _(sy):
            for t in range(4):
                sy.wait_ge(dv.h, dvE[(t, 7)][2])
                st.inc(sy.dma_start(
                    out=x1_out.rearrange("(t p) d -> p t d", p=P)[:, 4 * t:4 * t + 4, :],
                    in_=xbuf[:].rearrange("p (u d) -> p u d", u=4)), 16)
            sy.wait_ge(st.h, stO[3])
        blk4.__exit__(None, None, None)
    return nc



def build_mloc():
    nc = bass.Bass("TRN2", target_bir_lowering=False, debug=False)
    xn1 = nc.dram_tensor("xn1", [TC, D], F32, kind="ExternalInput").ap()
    mloc = nc.dram_tensor("mloc", [D, D], F32, kind="ExternalOutput").ap()
    with ExitStack() as es:
        e = es.enter_context
        xn_sb = e(nc.sbuf_tensor("xn_sb", [P, NT * D], BF16))
        mbuf = [e(nc.sbuf_tensor(f"mbuf{i}", [P, 512], F32)) for i in range(2)]
        ps = [e(nc.psum_tensor(f"ps{i}", [P, 512], F32)) for i in range(2)]
        ld = SemCtr(es, nc, "ld")
        pe = SemCtr(es, nc, "pe")
        dv = SemCtr(es, nc, "dv")
        st = SemCtr(es, nc, "st")
        block = e(nc.Block())

        m3 = mloc.rearrange("(d p) o -> p d o", p=P)

        @block.gpsimd
        def _(g):
            ld.inc(g.dma_start(out=xn_sb[:].rearrange("p (t d) -> p t d", t=NT),
                               in_=xn1.rearrange("(t p) d -> p t d", p=P)), 16)

        @block.tensor
        def _(t_):
            t_.wait_ge(ld.h, 16)
            for k in range(16):
                d1, half = divmod(k, 2)
                if k >= 2:
                    t_.wait_ge(dv.h, k - 1)
                for ti in range(NT):
                    inst = t_.matmul(ps[k % 2][:, 0:512],
                                     xn_sb[:, ti * D + d1 * P:ti * D + (d1 + 1) * P],
                                     xn_sb[:, ti * D + half * 512:ti * D + (half + 1) * 512],
                                     start=(ti == 0), stop=(ti == NT - 1))
                pe.inc(inst)

        @block.vector
        def _(v):
            for k in range(16):
                v.wait_ge(pe.h, k + 1)
                if k >= 2:
                    v.wait_ge(st.h, 16 * (k - 1))
                dv.inc(v.tensor_copy(mbuf[k % 2][:], ps[k % 2][:, 0:512]))

        @block.sync
        def _(sy):
            for k in range(16):
                d1, half = divmod(k, 2)
                sy.wait_ge(dv.h, k + 1)
                st.inc(sy.dma_start(out=m3[:, d1:d1 + 1, half * 512:(half + 1) * 512],
                                    in_=mbuf[k % 2][:].unsqueeze(1)), 16)
            sy.wait_ge(st.h, 256)
    return nc



def build_l1():
    nc = bass.Bass("TRN2", target_bir_lowering=False, debug=False)
    x1_in = nc.dram_tensor("x1", [TC, D], F32, kind="ExternalInput").ap()
    xn1fm = nc.dram_tensor("xn1fm", [D, TC], F32, kind="ExternalInput").ap()
    mparts = nc.dram_tensor("mparts", [4 * D, D], F32, kind="ExternalInput").ap()
    wq1_in = nc.dram_tensor("wq1", [D, D], F32, kind="ExternalInput").ap()
    wg_in = nc.dram_tensor("wg", [D, D], F32, kind="ExternalInput").ap()
    bg_in = nc.dram_tensor("bg", [1, D], F32, kind="ExternalInput").ap()
    we_in = nc.dram_tensor("we", [8 * D, D], F32, kind="ExternalInput").ap()
    be_in = nc.dram_tensor("be", [1, 8 * D], F32, kind="ExternalInput").ap()
    x2_out = nc.dram_tensor("x2", [TC, D], F32, kind="ExternalOutput").ap()

    with ExitStack() as es:
        e = es.enter_context
        xn_sb = e(nc.sbuf_tensor("xn_sb", [P, DC * TC], BF16))
        m_bf = e(nc.sbuf_tensor("m_bf", [P, DC * D], BF16))
        w2_sb = e(nc.sbuf_tensor("w2_sb", [P, DC * D], BF16))
        acc = e(nc.sbuf_tensor("acc", [P, NT * D], BF16))
        msel = e(nc.sbuf_tensor("msel", [P, 8 * NT], F32))
        lg = e(nc.sbuf_tensor("lg", [P, 2 * D], F32))
        sexp = e(nc.sbuf_tensor("sexp", [P, D], BF16))
        max8 = e(nc.sbuf_tensor("max8", [P, 8], F32))
        idx8 = e(nc.sbuf_tensor("idx8", [P, 8], U32))
        eidx_u = e(nc.sbuf_tensor("eidx_u", [P, 1], U32))
        seven = e(nc.sbuf_tensor("seven", [P, 1], U32))
        nmx_all = e(nc.sbuf_tensor("nmx_all", [P, NT], F32))
        sume_all = e(nc.sbuf_tensor("sume_all", [P, NT], F32))
        p_all = e(nc.sbuf_tensor("p_all", [P, NT], F32))
        eidx_all = e(nc.sbuf_tensor("eidx_all", [P, NT], F32))
        tmp16 = e(nc.sbuf_tensor("tmp16", [P, NT], F32))
        tmpe = e(nc.sbuf_tensor("tmpe", [P, 2 * 512], BF16))
        x1b = e(nc.sbuf_tensor("x1b", [P, 2 * D], F32))
        ones_row_bf = e(nc.sbuf_tensor("ones_row_bf", [1, P], BF16))
        psU = [e(nc.psum_tensor(f"psU{i}", [P, 512], F32)) for i in range(2)]
        ld = SemCtr(es, nc, "ld")
        ldw = SemCtr(es, nc, "ldw")
        gp = SemCtr(es, nc, "gp")
        pe = SemCtr(es, nc, "pe")
        dv = SemCtr(es, nc, "dv")
        ac = SemCtr(es, nc, "ac")
        st = SemCtr(es, nc, "st")

        with ExitStack() as es1:
            e1 = es1.enter_context
            msum = e1(nc.sbuf_tensor("msum", [P, DC * D], F32))
            mtmp = e1(nc.sbuf_tensor("mtmp", [P, DC * D], F32))
            wq1_bf = e1(nc.sbuf_tensor("wq1_bf", [P, DC * D], BF16))
            blk = nc.Block("h1"); blk.__enter__()
            mp3 = mparts.rearrange("(r c p) o -> p r c o", r=4, p=P)

            @blk.gpsimd
            def _(g):
                gp.inc(g.memset(ones_row_bf[:], 1.0))
                gp.inc(g.memset(seven[:], 7))
                ld.inc(g.dma_start(out=msum[:].rearrange("p (c o) -> p c o", c=DC),
                                   in_=mp3[:, 0]), 16)
                ld.inc(g.dma_start(out=mtmp[:].rearrange("p (c o) -> p c o", c=DC),
                                   in_=mp3[:, 1]), 16)
                g.wait_ge(dv.h, 1)
                ld.inc(g.dma_start(out=mtmp[:].rearrange("p (c o) -> p c o", c=DC),
                                   in_=mp3[:, 2]), 16)
                g.wait_ge(dv.h, 2)
                ld.inc(g.dma_start(out=mtmp[:].rearrange("p (c o) -> p c o", c=DC),
                                   in_=mp3[:, 3]), 16)
                ldw.inc(g.dma_start(out=wq1_bf[:].rearrange("p (c o) -> p c o", c=DC),
                                    in_=wq1_in.rearrange("(c p) o -> p c o", p=P)), 16)
                ldw.inc(g.dma_start(out=xn_sb[:].rearrange("p (c t) -> p c t", c=DC),
                                    in_=xn1fm.rearrange("(c p) t -> p c t", p=P)), 16)

            @blk.vector
            def _(v):
                v.wait_ge(ld.h, 32)
                dv.inc(v.tensor_add(msum[:], msum[:], mtmp[:]))
                v.wait_ge(ld.h, 48)
                v.wait_ge(dv.h, 1)
                dv.inc(v.tensor_add(msum[:], msum[:], mtmp[:]))
                v.wait_ge(ld.h, 64)
                v.wait_ge(dv.h, 2)
                dv.inc(v.tensor_add(msum[:], msum[:], mtmp[:]))
                v.wait_ge(dv.h, 3)
                dv.inc(v.tensor_copy(m_bf[:], msum[:]))

            @blk.tensor
            def _(t_):
                t_.wait_ge(ldw.h, 16)
                t_.wait_ge(dv.h, 4)
                for k in range(16):
                    hc, half = divmod(k, 2)
                    if k >= 2:
                        t_.wait_ge(dv.h, 4 + k - 1)
                    for oc in range(DC):
                        inst = t_.matmul(psU[k % 2][:, 0:512],
                                         wq1_bf[:, oc * D + hc * P:oc * D + (hc + 1) * P],
                                         m_bf[:, oc * D + half * 512:oc * D + (half + 1) * 512],
                                         start=(oc == 0), stop=(oc == DC - 1))
                    pe.inc(inst)

            @blk.vector
            def _(v):
                for k in range(16):
                    hc, half = divmod(k, 2)
                    v.wait_ge(pe.h, k + 1)
                    dv.inc(v.tensor_copy(w2_sb[:, hc * D + half * 512:hc * D + (half + 1) * 512],
                                         psU[k % 2][:, 0:512]))
            blk.__exit__(None, None, None)
        H2_pe = 16
        H2_dv = 20
        H2_gp = 2

        wg_bf = e(nc.sbuf_tensor("wg_bf", [P, DC * D], BF16))
        webf = [e(nc.sbuf_tensor(f"webf{i}", [P, DC * D], BF16)) for i in range(2)]
        bg_bf = e(nc.sbuf_tensor("bg_bf", [1, D], BF16))
        be_bf = e(nc.sbuf_tensor("be_bf", [1, 8 * D], BF16))
        dvN = {}
        n = H2_dv
        for t in range(NT):
            for k in ("ev0", "ev1", "max8", "nmx", "idx", "and", "eidx"):
                n += 1
                dvN[(k, t)] = n
        dvN["recip"] = n + 1
        n += 1
        for ee in range(8):
            dvN[("mseleq", ee)] = n + 1
            dvN[("mselmul", ee)] = n + 2
            n += 2
        H3_dv = n
        peN = {}
        n = H2_pe
        for t in range(NT):
            for half in range(2):
                n += 1
                peN[("lg", t, half)] = n
        H3_pe = n
        acN = {}
        n = 0
        for t in range(NT):
            n += 1
            acN[("exp", t)] = n
        H3_ac = n

        blk3 = nc.Block("h3"); blk3.__enter__()

        @blk3.gpsimd
        def _(g):
            ldw.inc(g.dma_start(out=wg_bf[:].rearrange("p (c o) -> p c o", c=DC),
                                in_=wg_in.rearrange("(c p) o -> p c o", p=P)), 16)
            ldw.inc(g.dma_start(out=bg_bf[:], in_=bg_in), 16)
            ldw.inc(g.dma_start(out=be_bf[:], in_=be_in), 16)

        @blk3.tensor
        def _(t_):
            t_.wait_ge(ldw.h, 80)
            t_.wait_ge(gp.h, 1)
            for t in range(NT):
                for half in range(2):
                    k = t * 2 + half
                    if k >= 2:
                        t2, h2 = divmod(k - 2, 2)
                        t_.wait_ge(dv.h, dvN[("ev" + str(h2), t2)])
                    for cc in range(DC):
                        t_.matmul(psU[k % 2][:, 0:512],
                                  xn_sb[:, cc * TC + t * P:cc * TC + (t + 1) * P],
                                  wg_bf[:, cc * D + half * 512:cc * D + (half + 1) * 512],
                                  start=(cc == 0), stop=False)
                    pe.inc(t_.matmul(psU[k % 2][:, 0:512], ones_row_bf[:],
                                     bg_bf[0:1, half * 512:(half + 1) * 512],
                                     start=False, stop=True))

        @blk3.vector
        def _(v):
            for t in range(NT):
                for half in range(2):
                    v.wait_ge(pe.h, peN[("lg", t, half)])
                    if t >= 2:
                        v.wait_ge(ac.h, acN[("exp", t - 2)])
                    dv.inc(v.tensor_copy(lg[:, (t % 2) * D + half * 512:(t % 2) * D + (half + 1) * 512],
                                         psU[(t * 2 + half) % 2][:, 0:512]))
                v.wait_ge(dv.h, dvN[("ev1", t)])
                dv.inc(v.max(max8[:], lg[:, (t % 2) * D:(t % 2 + 1) * D]))
                v.wait_ge(dv.h, dvN[("max8", t)])
                dv.inc(v.tensor_scalar(nmx_all[:, t:t + 1], max8[:, 0:1], -1.0, None,
                                       ALU.mult))
                dv.inc(v.max_index(idx8[:], max8[:], lg[:, (t % 2) * D:(t % 2 + 1) * D]))
                v.wait_ge(dv.h, dvN[("idx", t)])
                dv.inc(v.tensor_single_scalar(eidx_u[:], idx8[:, 0:1], seven[:],
                                              ALU.bitwise_and))
                v.wait_ge(dv.h, dvN[("and", t)])
                dv.inc(v.tensor_copy(eidx_all[:, t:t + 1], eidx_u[:]))
            v.wait_ge(ac.h, acN[("exp", NT - 1)])
            dv.inc(v.reciprocal(p_all[:], sume_all[:]))
            v.wait_ge(dv.h, dvN["recip"])
            for ee in range(8):
                dv.inc(v.tensor_scalar(tmp16[:], eidx_all[:], float(ee), None,
                                       ALU.is_equal))
                v.wait_ge(dv.h, dvN[("mseleq", ee)])
                dv.inc(v.tensor_mul(msel[:, ee * NT:(ee + 1) * NT], tmp16[:], p_all[:]))

        @blk3.scalar
        def _(sc):
            for t in range(NT):
                sc.wait_ge(dv.h, dvN[("nmx", t)])
                ac.inc(sc.activation(sexp[:], lg[:, (t % 2) * D:(t % 2 + 1) * D],
                                     AF.Exp, bias=nmx_all[:, t:t + 1],
                                     accum_out=sume_all[:, t:t + 1]))
        blk3.__exit__(None, None, None)

        peE = {}
        n = H3_pe
        for ee in range(8):
            for t in range(NT):
                for half in range(2):
                    n += 1
                    peE[(ee, t, half)] = n
        H4_pe = n
        dvA = {}
        n = H3_dv
        for ee in range(8):
            for t in range(NT):
                for half in range(2):
                    dvA[(ee, t, half)] = (n + 1, n + 2)
                    n += 2
        H4_dv = n
        ldE = {ee: 80 + 16 * (ee + 1) for ee in range(8)}

        blk4 = nc.Block("h4"); blk4.__enter__()
        we3 = we_in.rearrange("(ee c p) o -> p ee c o", ee=8, p=P)

        @blk4.gpsimd
        def _(g):
            for ee in range(8):
                if ee >= 2:
                    g.wait_ge(pe.h, peE[(ee - 2, NT - 1, 1)])
                ldw.inc(g.dma_start(out=webf[ee % 2][:].rearrange("p (c o) -> p c o", c=DC),
                                    in_=we3[:, ee]), 16)

        @blk4.tensor
        def _(t_):
            for ee in range(8):
                t_.wait_ge(ldw.h, ldE[ee])
                for t in range(NT):
                    for half in range(2):
                        k = (ee * NT + t) * 2 + half
                        if k >= 2:
                            kk = k - 2
                            e2, r = divmod(kk, NT * 2)
                            t2, h2 = divmod(r, 2)
                            t_.wait_ge(dv.h, dvA[(e2, t2, h2)][0])
                        for cc in range(DC):
                            t_.matmul(psU[k % 2][:, 0:512],
                                      xn_sb[:, cc * TC + t * P:cc * TC + (t + 1) * P],
                                      webf[ee % 2][:, cc * D + half * 512:cc * D + (half + 1) * 512],
                                      start=(cc == 0), stop=False)
                        pe.inc(t_.matmul(psU[k % 2][:, 0:512], ones_row_bf[:],
                                         be_bf[0:1, ee * D + half * 512:ee * D + (half + 1) * 512],
                                         start=False, stop=True))

        @blk4.vector
        def _(v):
            for ee in range(8):
                for t in range(NT):
                    for half in range(2):
                        k = (ee * NT + t) * 2 + half
                        par = k % 2
                        r1, r2 = dvA[(ee, t, half)]
                        v.wait_ge(pe.h, peE[(ee, t, half)])
                        dv.inc(v.tensor_scalar(tmpe[:, par * 512:(par + 1) * 512],
                                               psU[par][:, 0:512],
                                               msel[:, ee * NT + t:ee * NT + t + 1],
                                               0.0, ALU.mult, ALU.max))
                        v.wait_ge(dv.h, r1)
                        sl = acc[:, t * D + half * 512:t * D + (half + 1) * 512]
                        if ee == 0:
                            dv.inc(v.tensor_copy(sl, tmpe[:, par * 512:(par + 1) * 512]))
                        else:
                            v.wait_ge(dv.h, dvA[(ee - 1, t, half)][1])
                            dv.inc(v.tensor_add(sl, sl, tmpe[:, par * 512:(par + 1) * 512]))
        blk4.__exit__(None, None, None)

        peA = {}
        n = H4_pe
        for t in range(NT):
            for half in range(2):
                n += 1
                peA[(t, half)] = n
        dvC = {}
        n = H4_dv
        for t in range(NT):
            for half in range(2):
                dvC[(t, half)] = (n + 1, n + 2)
                n += 2
        H5_dv = n
        ldX = {t: 16 * (t + 1) for t in range(NT)}
        LD_B = 64

        blk5 = nc.Block("h5"); blk5.__enter__()
        x1_3 = x1_in.rearrange("(t p) d -> p t d", p=P)
        x2_3 = x2_out.rearrange("(t p) d -> p t d", p=P)

        @blk5.gpsimd
        def _(g):
            for t in range(NT):
                if t >= 2:
                    g.wait_ge(st.h, 16 * (t - 1))
                ld.inc(g.dma_start(out=x1b[:, (t % 2) * D:(t % 2 + 1) * D].unsqueeze(1),
                                   in_=x1_3[:, t:t + 1, :]), 16)

        @blk5.tensor
        def _(t_):
            t_.wait_ge(dv.h, H2_dv)
            for t in range(NT):
                for half in range(2):
                    k = t * 2 + half
                    if k >= 2:
                        t2, h2 = divmod(k - 2, 2)
                        t_.wait_ge(dv.h, dvC[(t2, h2)][0])
                    for cc in range(DC):
                        inst = t_.matmul(psU[k % 2][:, 0:512],
                                         xn_sb[:, cc * TC + t * P:cc * TC + (t + 1) * P],
                                         w2_sb[:, cc * D + half * 512:cc * D + (half + 1) * 512],
                                         start=(cc == 0), stop=(cc == DC - 1))
                    pe.inc(inst)

        @blk5.vector
        def _(v):
            for t in range(NT):
                v.wait_ge(ld.h, LD_B + ldX[t])
                for half in range(2):
                    c1, c2 = dvC[(t, half)]
                    v.wait_ge(pe.h, peA[(t, half)])
                    sl = x1b[:, (t % 2) * D + half * 512:(t % 2) * D + (half + 1) * 512]
                    dv.inc(v.tensor_add(sl, sl, psU[(t * 2 + half) % 2][:, 0:512]))
                    v.wait_ge(dv.h, c1)
                    dv.inc(v.tensor_add(sl, sl,
                                        acc[:, t * D + half * 512:t * D + (half + 1) * 512]))

        @blk5.sync
        def _(sy):
            for t in range(NT):
                sy.wait_ge(dv.h, dvC[(t, 1)][1])
                st.inc(sy.dma_start(out=x2_3[:, t:t + 1, :],
                                    in_=x1b[:, (t % 2) * D:(t % 2 + 1) * D].unsqueeze(1)), 16)
            sy.wait_ge(st.h, 16 * NT)
        blk5.__exit__(None, None, None)
    return nc



def _shard_rows(x):
    out = []
    for b in range(2):
        for s in range(4):
            out.append(np.ascontiguousarray(x[b, s * TC:(s + 1) * TC, :]))
    return out


def kernel(**inputs):
    cores = list(range(8))
    x = np.ascontiguousarray(inputs["L0_attn_hidden_states"], dtype=np.float32)
    slabs = _shard_rows(x)

    Wq = inputs["L0_attn_Wq"]; Wk = inputs["L0_attn_Wk"]; Wv = inputs["L0_attn_Wv"]
    Wg1 = inputs["L0_attn_Wg1"]; Wg2 = inputs["L0_attn_Wg2"]
    gnorm = inputs["L0_attn_g_norm_weight"]; Wo = inputs["L0_attn_Wo"]
    Wfc = inputs["L0_ffn_W_fc"]; lnw = inputs["L0_ffn_ln_w"]
    lnb = inputs["L0_ffn_ln_b"]; Wp = inputs["L0_ffn_W_proj"]
    Wq1 = inputs["L1_attn_W_q"]
    Wg = inputs["L1_ffn_Wg"]; bg = inputs["L1_ffn_bg"]
    We = inputs["L1_ffn_We"]; be = inputs["L1_ffn_be"]

    def T(a):
        return np.ascontiguousarray(np.asarray(a).T)

    res = run_bass_kernel_spmd(build_ln(), [{"xin": s} for s in slabs], core_ids=cores)
    xn0 = [res.results[i]["xn"] for i in range(8)]

    wqT, wkT, wvT, wg1T, wg2T = T(Wq), T(Wk), T(Wv), T(Wg1), T(Wg2)
    in_maps = [{"xnfm": T(xn0[i]), "wqT": wqT, "wkT": wkT, "wvT": wvT,
                "wg1T": wg1T, "wg2T": wg2T} for i in range(8)]
    res = run_bass_kernel_spmd(build_proj0(), in_maps, core_ids=cores)
    B = [res.results[i] for i in range(8)]

    in_maps = [{"a_fm": B[i]["a"], "a_tm": T(B[i]["a"]), "v_tm": B[i]["v"]}
               for i in range(8)]
    res = run_bass_kernel_spmd(build_ploc(), in_maps, core_ids=cores)
    C = [res.results[i] for i in range(8)]

    gn_pc = np.ascontiguousarray(np.asarray(gnorm).reshape(DC, P).T)
    woT = T(Wo)
    zero_p = np.zeros((D, DK), np.float32)
    zero_z = np.zeros((P, NHEAD), np.float32)
    in_maps = []
    for i in range(8):
        r = i % 4
        pp = [C[i - r + j]["ploc"] if j < r else zero_p for j in range(3)]
        zp = [C[i - r + j]["zloc"] if j < r else zero_z for j in range(3)]
        in_maps.append({"q": B[i]["q"], "a_fm": B[i]["a"], "a_tm": T(B[i]["a"]),
                        "v_tm": B[i]["v"], "sg": B[i]["sg"], "gnorm_pc": gn_pc,
                        "woT": woT, "pprev": np.ascontiguousarray(np.stack(pp)),
                        "zprev": np.ascontiguousarray(np.stack(zp))})
    res = run_bass_kernel_spmd(build_attn(), in_maps, core_ids=cores)
    attn0 = [res.results[i]["attn0"] for i in range(8)]

    wfcT, wpT = T(Wfc), T(Wp)
    lnw_pc = np.ascontiguousarray(np.asarray(lnw).reshape(HC, P).T)
    lnb_pc = np.ascontiguousarray(np.asarray(lnb).reshape(HC, P).T)
    in_maps = [{"x": slabs[i], "attn0": attn0[i], "xnfm": T(xn0[i]),
                "wfcT": wfcT, "wpT": wpT, "lnw_pc": lnw_pc, "lnb_pc": lnb_pc}
               for i in range(8)]
    res = run_bass_kernel_spmd(build_ffn0(), in_maps, core_ids=cores)
    x1 = [res.results[i]["x1"] for i in range(8)]

    res = run_bass_kernel_spmd(build_ln(), [{"xin": x1[i]} for i in range(8)],
                               core_ids=cores)
    xn1 = [res.results[i]["xn"] for i in range(8)]

    res = run_bass_kernel_spmd(build_mloc(), [{"xn1": xn1[i]} for i in range(8)],
                               core_ids=cores)
    mloc = [res.results[i]["mloc"] for i in range(8)]

    wq1 = np.ascontiguousarray(np.asarray(Wq1, dtype=np.float32))
    wg = np.ascontiguousarray(np.asarray(Wg, dtype=np.float32))
    bg1 = np.ascontiguousarray(np.asarray(bg, dtype=np.float32).reshape(1, D))
    we_flat = np.ascontiguousarray(np.asarray(We, dtype=np.float32).reshape(8 * D, D))
    be1 = np.ascontiguousarray(np.asarray(be, dtype=np.float32).reshape(1, 8 * D))
    in_maps = []
    for i in range(8):
        g0 = (i // 4) * 4
        mp = np.ascontiguousarray(np.concatenate([mloc[g0 + j] for j in range(4)], axis=0))
        in_maps.append({"x1": x1[i], "xn1fm": T(xn1[i]), "mparts": mp,
                        "wq1": wq1, "wg": wg, "bg": bg1, "we": we_flat, "be": be1})
    res = run_bass_kernel_spmd(build_l1(), in_maps, core_ids=cores)
    x2 = [res.results[i]["x2"] for i in range(8)]

    out = np.empty((2, 8192, D), np.float32)
    for b in range(2):
        for sidx in range(4):
            out[b, sidx * TC:(sidx + 1) * TC, :] = x2[b * 4 + sidx]
    return out
